# version 14
# speedup vs baseline: 1.3868x; 1.0074x over previous
import numpy as np
import concourse.bass as bass
import concourse.mybir as mybir
from concourse.bass_utils import run_bass_kernel_spmd
from contextlib import ExitStack

F32 = mybir.dt.float32
BF16 = mybir.dt.bfloat16
I8 = mybir.dt.int8
AF = mybir.ActivationFunctionType
ALU = mybir.AluOpType
AX = mybir.AxisListType

NCORE = 8
S = 2
N = 2048
D = 1024
H = 16
DH = 64
INNER = 4096
NT = N // 128
ND = D // 128
QB = 512
NQB = N // QB
EPS_LN = 1e-5
EPS_RMS = float(np.finfo(np.float32).eps)
VP = DH + 1
CRND = 12582912.0

_DRY = [False]


class LD(dict):
    def __missing__(self, k):
        if _DRY[0]:
            return 0
        raise KeyError(k)


class DryInstr:
    def then_inc(self, h, n):
        return self


class DryEng:
    def __getattr__(self, name):
        def f(*a, **kw):
            return DryInstr()
        return f


INSTR_INCS = {}


class Sem:
    def __init__(self, stack, nc, name):
        self.h = stack.enter_context(nc.semaphore(name))
        self.n = 0

    def inc(self, instr, k=1):
        instr.then_inc(self.h, k)
        self.n += k
        INSTR_INCS[id(instr)] = (instr, self, self.n)
        return self.n


AUTO_WINDOW = 3

_READ_KEYS = ("in_", "in0", "in1", "scalar", "scalar1", "scalar2", "bias",
              "scale", "alpha", "identity", "mask", "data", "on_true", "on_false")


def _ap_region(ap):
    try:
        t = ap.tensor.name
        off = ap.offset
        lo = off
        hi = off
        dims = list(ap.ap)[1:]
        for step, count in dims:
            if step >= 0:
                hi += step * (count - 1)
            else:
                lo += step * (count - 1)
        return (t, lo, hi + 1)
    except Exception:
        return None


def _overlap(r1, r2):
    return r1[0] == r2[0] and r1[1] < r2[2] and r2[1] < r1[2]


class AutoEng:
    _SKIP = {"wait_ge", "sem_inc", "engine_nop", "nop", "dma_start", "drain",
             "dma_start_transpose", "wait_op", "memzero"}

    def __init__(self, eng, st, sem):
        self._e = eng
        self._st = st
        self._sem = sem

    def __getattr__(self, name):
        real = getattr(self._e, name)
        if name in self._SKIP or name.startswith("_") or not callable(real):
            return real

        def call(*args, **kw):
            writes = []
            reads = []
            for k, v in kw.items():
                r = _ap_region(v) if v is not None else None
                if r is None:
                    continue
                if k in ("out", "accum_out", "ap"):
                    writes.append(r)
                elif k in _READ_KEYS:
                    reads.append(r)
            if args and name == "memset":
                r = _ap_region(args[0])
                if r:
                    writes.append(r)
            elif args:
                for v in args:
                    r = _ap_region(v)
                    if r:
                        reads.append(r)
            live = self._st["live"]
            hit = False
            for (pr, pw, pi) in live:
                for r in reads:
                    if any(_overlap(r, w) for w in pw):
                        hit = True
                for w in writes:
                    if any(_overlap(w, x) for x in pw) or any(_overlap(w, x) for x in pr):
                        hit = True
                if hit:
                    break
            if hit and live:
                last = live[-1][2]
                rec = INSTR_INCS.get(id(last))
                if rec is not None:
                    self._e.wait_ge(rec[1].h, rec[2])
                else:
                    n = self._sem.inc(last)
                    self._e.wait_ge(self._sem.h, n)
                self._st["barriers"] += 1
                live.clear()
            ii = real(*args, **kw)
            live.append((reads, writes, ii))
            if len(live) > AUTO_WINDOW:
                live.pop(0)
            return ii
        return call


def _alias(nc, name, shape, dtype, base_handle, delta_bytes):
    h = nc.virtual_sbuf_tensor(name, shape, dtype)
    nc._tensor(h.name, list(h.shape), h.dtype, type="SB")
    mloc = nc.lookup_mloc(h)
    mloc.allocated = True
    mloc.addr = nc.lookup_mloc(base_handle).addr + delta_bytes
    return h


def build_program(upto="full", dbg=False):
    INSTR_INCS.clear()
    nc = bass.Bass("TRN2", target_bir_lowering=False, detect_race_conditions=False)
    stack = ExitStack()

    x_in = nc.dram_tensor("x", [S, N, D], F32, kind="ExternalInput").ap()
    y_out = nc.dram_tensor("y", [S, N, D], F32, kind="ExternalOutput").ap()
    R = nc.dram_tensor("Rscratch", [S, N, D], F32, kind="Internal").ap()
    w_in = LD()
    for nm, shp in [
        ("l0_wqT", [D, D]), ("l0_wkT", [D, D]), ("l0_wvT", [D, D]), ("l0_woT", [D, D]),
        ("l0_qnw", [1, DH]), ("l0_knw", [1, DH]),
        ("l1_vT", [D, 256]), ("l1_uT", [256, 3 * D]),
        ("l1_ovT", [D, 256]), ("l1_ouT", [256, D]),
        ("g0T", [D, INNER]), ("u0T", [D, INNER]), ("d0T", [INNER, D]),
        ("g1T", [D, INNER]), ("u1T", [D, INNER]), ("d1T", [INNER, D]),
    ]:
        w_in[nm] = nc.dram_tensor(nm, shp, F32, kind="ExternalInput").ap()
    q8 = LD()
    for l in range(2):
        q8[(l, "g")] = nc.dram_tensor(f"g8_{l}", [8, 8, 128, 512], BF16, kind="Internal").ap()
        q8[(l, "u")] = nc.dram_tensor(f"u8_{l}", [8, 8, 128, 512], BF16, kind="Internal").ap()
        q8[(l, "d")] = nc.dram_tensor(f"d8_{l}", [2, 32, 128, 512], BF16, kind="Internal").ap()

    arenaA = nc.alloc_sbuf_tensor("arenaA", [128, 32768], mybir.dt.uint8)
    arenaB = nc.alloc_sbuf_tensor("arenaB", [128, 98816], mybir.dt.uint8)
    arenaR = nc.alloc_sbuf_tensor("arenaR", [128, 49152], mybir.dt.uint8)
    A_ = lambda nm, shp, dt, off: _alias(nc, nm, shp, dt, arenaA, off).ap()
    B_ = lambda nm, shp, dt, off: _alias(nc, nm, shp, dt, arenaB, off).ap()
    R_ = lambda nm, shp, dt, off: _alias(nc, nm, shp, dt, arenaR, off).ap()

    xhT = A_("xhT", [128, ND * N], BF16, 0)
    xiT = A_("xiT", [128, ND * 512], BF16, 0)
    x2qbf = A_("x2qbf", [128, 2 * 1024], BF16, 8192)
    silt = A_("silt", [128, 2 * 512], BF16, 12288)
    fmisc = A_("fmisc", [128, 2 * 1024], F32, 16384)

    qkhT = B_("qkhT", [128, 16 * N], BF16, 0)
    v_t = B_("v_t", [128, NT * H * VP], BF16, 65536)
    h_t = B_("h_t", [128, 4 * INNER], BF16, 0)
    x2bf = B_("x2bf", [128, 4 * INNER], BF16, 32768)
    x2T = B_("x2T", [128, 32 * 512], BF16, 65536)
    pbuf = B_("pbuf", [128, 8 * 1024], F32, 0)
    pbuf2 = B_("pbuf2", [128, 4 * 1024], F32, 32768)
    qf1 = B_("qf1", [128, 2048], F32, 49152)
    x2q2 = B_("x2q2", [128, 2 * 2048], BF16, 57344)

    wq_sb = R_("wq_sb", [128, 8 * D], BF16, 0)
    wk_sb = R_("wk_sb", [128, 8 * D], BF16, 16384)
    wv_sb = R_("wv_sb", [128, 8 * D], BF16, 32768)
    PT = R_("PT", [128, NT * QB], BF16, 0)
    wo_sb = R_("wo_sb", [128, 8 * D], BF16, 16384)
    l1v_sb = R_("l1v_sb", [128, 8 * 256], BF16, 32768)
    l1u_sb = R_("l1u_sb", [128, 2 * 3 * D], BF16, 36864)
    l1ov_sb = R_("l1ov_sb", [128, 8 * 256], BF16, 16384)
    l1ou_sb = R_("l1ou_sb", [128, 2 * D], BF16, 20480)
    rbuf2 = R_("rbuf2", [128, 2 * D], F32, 0)
    wg_sb = R_("wg_sb", [128, 2 * 8 * 512], BF16, 0)
    wu_sb = R_("wu_sb", [128, 2 * 8 * 512], BF16, 16384)
    wd_sb = R_("wd_sb", [128, 2 * 8 * 512], BF16, 32768)

    sb = lambda nm, shp, dt: stack.enter_context(nc.sbuf_tensor(nm, shp, dt)).ap()
    rbuf = sb("rbuf", [128, 2 * D], F32)
    robuf = sb("robuf", [128, 2 * D], F32)
    xbt = sb("xbt", [128, 2 * D], BF16)
    sqt = sb("sqt", [128, D], F32)
    aptmp = sqt
    tsb = sb("tsb", [128, 2 * 512], BF16)
    t2sb = tsb
    ident = sb("ident", [128, 128], BF16)
    i16 = sb("i16", [16, 16], BF16)
    onesr_f = sb("onesr_f", [1, 128], F32)
    onesr_b = sb("onesr_b", [1, 128], BF16)
    ones64 = sb("ones64", [1, 64], BF16)
    onescol_f = sb("onescol_f", [128, 1], F32)
    bd2 = sb("bd2", [128, 2], BF16)
    eps_ln = sb("eps_ln", [128, 1], F32)
    eps_rms = sb("eps_rms", [128, 1], F32)
    mean_all = sb("mean_all", [128, 16], F32)
    rstd_all = sb("rstd_all", [128, 16], F32)
    stat_s = sb("stat_s", [128, 16], F32)
    stat_q = sb("stat_q", [128, 16], F32)
    smisc = sb("smisc", [128, 16], F32)
    rmsbuf = sb("rmsbuf", [128, 64], F32)
    wqbc = sb("wqbc", [128, DH], BF16)
    wkbc = sb("wkbc", [128, DH], BF16)
    gnr = sb("gnr", [1, 3 * DH], F32)
    normrow = gnr
    bias_l0 = sb("bias_l0", [128, 1], F32)
    bias_l1 = sb("bias_l1", [128, 16], F32)
    crow_b = sb("crow_b", [1, 16], BF16)
    rrow = sb("rrow", [1, QB], BF16)
    rb_sb = sb("rb_sb", [64, QB], BF16)
    swcol = sb("swcol", [128, 1], F32)
    swmeans = sb("swmeans", [128, 8], F32)
    sw1 = sb("sw1", [1, 8], F32)
    qcol = sb("qcol", [128, 12], F32)

    pq = stack.enter_context(nc.psum_tensor("pq", [128, 1024], F32)).ap()
    pk = stack.enter_context(nc.psum_tensor("pk", [128, 1024], F32)).ap()
    pv = stack.enter_context(nc.psum_tensor("pv", [128, 1024], F32)).ap()
    pt = stack.enter_context(nc.psum_tensor("pt", [128, 1024], F32)).ap()

    s_mm = Sem(stack, nc, "s_mm")
    s_ve = Sem(stack, nc, "s_ve")
    s_se = Sem(stack, nc, "s_se")
    s_gp = Sem(stack, nc, "s_gp")
    s_xa = Sem(stack, nc, "s_xa")
    s_xb = Sem(stack, nc, "s_xb")
    s_oa = Sem(stack, nc, "s_oa")
    s_ob = Sem(stack, nc, "s_ob")
    s_wg = Sem(stack, nc, "s_wg")
    s_ws = Sem(stack, nc, "s_ws")
    s_wl = Sem(stack, nc, "s_wl")
    SEMS = [s_mm, s_ve, s_se, s_gp, s_xa, s_xb, s_oa, s_ob, s_wg, s_ws, s_wl]

    PROG = []

    def phase(eng):
        def deco(fn):
            PROG.append((eng, fn))
            return fn
        return deco

    gp0 = LD()

    @phase("gpsimd")
    def p0_gp(g):
        for instr in [
            g.memset(ident[:], 1.0),
            g.memset(i16[:], 1.0),
            g.memset(bd2[:], 0.0),
            g.memset(onesr_f[:], 1.0),
            g.memset(onesr_b[:], 1.0),
            g.memset(ones64[:], 1.0),
            g.memset(onescol_f[:], 1.0),
            g.memset(eps_ln[:], EPS_LN),
            g.memset(eps_rms[:], EPS_RMS),
            g.memset(v_t.rearrange("p (t h c) -> p t h c", h=H, c=VP)[:, :, :, DH:VP], 1.0),
        ]:
            s_gp.inc(instr)
        g.wait_ge(s_gp.h, s_gp.n)
        for instr in [
            g.affine_select(out=ident[:], in_=ident[:], pattern=[[1, 128]],
                            compare_op=ALU.is_equal, fill=0.0, base=0, channel_multiplier=-1),
            g.affine_select(out=i16[:], in_=i16[:], pattern=[[1, 16]],
                            compare_op=ALU.is_equal, fill=0.0, base=0, channel_multiplier=-1),
            g.memset(bd2[0:64, 0:1], 1.0),
            g.memset(bd2[64:128, 1:2], 1.0),
        ]:
            s_gp.inc(instr)
        gp0["c"] = s_gp.n

    p0d = LD()

    @phase("sync")
    def p0_sync(sy):
        s_ws.inc(sy.dma_start(out=gnr[:, 0:DH], in_=w_in["l0_qnw"][:, :]), 16)
        p0d["gnr"] = s_ws.inc(sy.dma_start(out=gnr[:, DH:2 * DH], in_=w_in["l0_knw"][:, :]), 16)

    @phase("vector")
    def p0_ve(v):
        v.wait_ge(s_ws.h, p0d["gnr"])
        v.tensor_tensor(out=gnr[:, 2 * DH:3 * DH], in0=gnr[:, 0:DH], in1=gnr[:, DH:2 * DH], op=ALU.mult)
        nbar = s_ve.inc(v.tensor_reduce(out=sw1[:, 0:1], in_=gnr[:, 2 * DH:3 * DH],
                                        axis=AX.X, op=ALU.max, apply_absolute_value=True))
        v.wait_ge(s_ve.h, nbar)
        v.tensor_scalar(out=sw1[:, 1:2], in0=sw1[:, 0:1], scalar1=-8.0, scalar2=0.0,
                        op0=ALU.mult, op1=ALU.add)
        v.tensor_copy(out=rrow[0:1, 0:DH], in_=gnr[:, 0:DH])
        p0d["ve"] = s_ve.inc(v.tensor_copy(out=rrow[0:1, DH:2 * DH], in_=gnr[:, DH:2 * DH]))

    @phase("tensor")
    def p0_mm(t):
        t.wait_ge(s_gp.h, gp0["c"])
        t.wait_ge(s_ve.h, p0d["ve"])
        t.matmul(pt[0:128, 0:1], onesr_f[:], sw1[0:1, 1:2], start=True, stop=True)
        t.matmul(pt[0:128, 128:192], onesr_b[:], rrow[0:1, 0:DH], start=True, stop=True)
        p0d["mm"] = s_mm.inc(t.matmul(pt[0:128, 192:256], onesr_b[:], rrow[0:1, DH:2 * DH],
                                      start=True, stop=True))

    @phase("vector")
    def p0_ve2(v):
        v.wait_ge(s_mm.h, p0d["mm"])
        v.tensor_copy(out=bias_l0[:], in_=pt[0:128, 0:1])
        v.tensor_copy(out=wqbc[:], in_=pt[0:128, 128:192])
        p0d["done"] = s_ve.inc(v.tensor_copy(out=wkbc[:], in_=pt[0:128, 192:256]))

    def emit_ffn_prep(l, PREPPREV=[None]):
        for mi, (mm_, wname, nrow, ncolq) in enumerate([("g", f"g{l}T", 8, 4), ("u", f"u{l}T", 8, 4),
                                                        ("d", f"d{l}T", 32, 1)]):
            mcol = l * 3 + mi
            src = w_in[wname]
            scratch = q8[(l, mm_)]
            nchunk = nrow * ncolq
            ve_done = LD()
            ldn = LD()
            ent = LD()

            ngrp = nchunk // 4

            @phase("sync")
            def ld1(sy, src=src, nrow=nrow, ncolq=ncolq, ve_done=ve_done, ldn=ldn,
                    ent=ent, pprev=PREPPREV[0], ngrp=ngrp):
                ent["v"] = s_ve.n
                ci = 0
                for r in range(nrow):
                    for q_ in range(ncolq):
                        if ci >= 8:
                            sy.wait_ge(s_ve.h, ve_done[ci // 4 - 2])
                        elif pprev is not None and ci < 4:
                            sy.wait_ge(s_ve.h, pprev[pprev["ngrp"] - 2])
                        elif pprev is not None:
                            sy.wait_ge(s_ve.h, pprev[pprev["ngrp"] - 1])
                        elif ci < 8:
                            sy.wait_ge(s_ve.h, ent["v"])
                        ldn[ci] = s_ws.inc(sy.dma_start(
                            out=pbuf.rearrange("p (s w) -> p s w", s=8)[:, ci % 8, :],
                            in_=src[r * 128:(r + 1) * 128, q_ * 1024:(q_ + 1) * 1024]), 16)
                        ci += 1

            @phase("vector")
            def st1(v, ve_done=ve_done, ldn=ldn, ngrp=ngrp):
                ve_done["ngrp"] = ngrp
                for g_ in range(ngrp):
                    v.wait_ge(s_ws.h, ldn[4 * g_ + 3])
                    ve_done[g_] = s_ve.inc(v.tensor_reduce(
                        out=rmsbuf[:, g_:g_ + 1],
                        in_=pbuf.rearrange("p (s w) -> p s w", s=2)[:, g_ % 2, :],
                        axis=AX.X, op=ALU.add, apply_absolute_value=True))
                v.wait_ge(s_ve.h, ve_done[ngrp - 1])
                ve_done["fin"] = s_ve.inc(v.tensor_reduce(
                    out=smisc[:, 8:9], in_=rmsbuf[:, 0:ngrp], axis=AX.X, op=ALU.add))

            d2 = LD()

            @phase("tensor")
            def red_mm(t, ve_done=ve_done, d2=d2):
                t.wait_ge(s_ve.h, ve_done["fin"])
                d2["mm"] = s_mm.inc(t.matmul(pt[0:1, 512:513], smisc[:, 8:9],
                                             onescol_f[:], start=True, stop=True))

            @phase("vector")
            def mk_scale(v, d2=d2):
                v.wait_ge(s_mm.h, d2["mm"])
                v.tensor_scalar(out=sw1[:, 2:3], in0=pt[0:1, 512:513],
                                scalar1=1.0 / (D * INNER), scalar2=1e-5, op0=ALU.mult, op1=ALU.max)
                d2["ve"] = s_ve.inc(v.reciprocal(out=sw1[:, 3:4], in_=sw1[:, 2:3]))

            @phase("tensor")
            def bc_mm(t, d2=d2):
                t.wait_ge(s_ve.h, d2["ve"])
                t.matmul(pt[0:128, 513:514], onesr_f[:], sw1[0:1, 3:4], start=True, stop=True)
                d2["mm2"] = s_mm.inc(t.matmul(pt[0:128, 514:515], onesr_f[:], sw1[0:1, 2:3],
                                              start=True, stop=True))

            @phase("vector")
            def cp_sc(v, d2=d2, mcol=mcol):
                v.wait_ge(s_mm.h, d2["mm2"])
                v.tensor_copy(out=swmeans[:, mcol:mcol + 1], in_=pt[0:128, 514:515])
                d2["sw"] = s_ve.inc(v.tensor_copy(out=swcol[:], in_=pt[0:128, 513:514]))

            qn = LD()
            ld2n = LD()
            st2n = LD()

            @phase("scalar")
            def ld2(se, src=src, nrow=nrow, ncolq=ncolq, qn=qn, ld2n=ld2n, d2=d2,
                    nchunk=nchunk):
                rq = [(r, q_) for r in range(nrow) for q_ in range(ncolq)]
                for ci in range(nchunk):
                    r, q_ = rq[ci]
                    se.wait_ge(s_ve.h, qn[ci // 2 - 2] if ci >= 4 else d2["sw"])
                    ld2n[ci] = s_wl.inc(se.dma_start(
                        out=pbuf2.rearrange("p (s w) -> p s w", s=4)[:, ci % 4, :],
                        in_=src[r * 128:(r + 1) * 128, q_ * 1024:(q_ + 1) * 1024]), 16)

            @phase("gpsimd")
            def st_gp(g, qn=qn, st2n=st2n, scratch=scratch, mm_=mm_, nchunk=nchunk,
                      nrow=nrow, ncolq=ncolq):
                rq = [(r, q_) for r in range(nrow) for q_ in range(ncolq)]
                for cj in range(nchunk):
                    rj, qj = rq[cj]
                    g.wait_ge(s_ve.h, qn[cj // 2])
                    if mm_ in ("g", "u"):
                        dstq = scratch[2 * qj:2 * qj + 2, rj, :, :]
                    else:
                        dstq = scratch[0:2, rj, :, :]
                    st2n[cj] = s_wg.inc(g.dma_start(
                        out=dstq.rearrange("h p f -> p h f"),
                        in_=x2q2.rearrange("p (s c h f) -> p s c h f", s=2, c=2,
                                           h=2)[:, (cj // 2) % 2, cj % 2]), 16)

            @phase("vector")
            def qz2(v, nchunk=nchunk, qn=qn, ld2n=ld2n, st2n=st2n, d2=d2):
                v.wait_ge(s_ve.h, d2["sw"])
                for p_ in range(nchunk // 2):
                    v.wait_ge(s_wl.h, ld2n[2 * p_ + 1])
                    if p_ >= 2:
                        v.wait_ge(s_wg.h, st2n[2 * (p_ - 2) + 1])
                    srcap = pbuf2.rearrange("p (s w) -> p s w", s=2)[:, p_ % 2, :]
                    v.tensor_scalar(out=fmisc[:], in0=srcap, scalar1=swcol[:, 0:1],
                                    scalar2=CRND, op0=ALU.mult, op1=ALU.add)
                    v.tensor_scalar(out=qf1[:], in0=fmisc[:], scalar1=CRND, scalar2=1.0,
                                    op0=ALU.subtract, op1=ALU.min)
                    qn[p_] = s_ve.inc(v.tensor_scalar(
                        out=x2q2.rearrange("p (s w) -> p s w", s=2)[:, p_ % 2, :],
                        in0=qf1[:], scalar1=-1.0, scalar2=0.0, op0=ALU.max, op1=ALU.add))

            PREPPREV[0] = ve_done

    def stats_cols(v, src_ap, tc):
        v.tensor_reduce(out=stat_s[:, tc:tc + 1], in_=src_ap, axis=AX.X, op=ALU.add)
        v.tensor_tensor(out=sqt[:], in0=src_ap, in1=src_ap, op=ALU.mult)
        return v.tensor_reduce(out=stat_q[:, tc:tc + 1], in_=sqt[:], axis=AX.X, op=ALU.add)

    def emit_stats_finish():
        dd = LD()

        @phase("vector")
        def f1(v, dd=dd):
            v.wait_ge(s_ve.h, s_ve.n)
            v.tensor_scalar(out=mean_all[:], in0=stat_s[:], scalar1=1.0 / D, scalar2=0.0,
                            op0=ALU.mult, op1=ALU.add)
            v.tensor_scalar(out=rstd_all[:], in0=stat_q[:], scalar1=1.0 / D, scalar2=0.0,
                            op0=ALU.mult, op1=ALU.add)
            v.tensor_tensor(out=stat_s[:], in0=mean_all[:], in1=mean_all[:], op=ALU.mult)
            dd["v"] = s_ve.inc(v.tensor_sub(out=stat_q[:], in0=rstd_all[:], in1=stat_s[:]))

        @phase("scalar")
        def f2(se, dd=dd):
            se.wait_ge(s_gp.h, gp0["c"])
            se.wait_ge(s_ve.h, dd["v"])
            nb2 = s_se.inc(se.activation(out=stat_s[:], in_=stat_q[:], func=AF.Ln, bias=eps_ln[:]))
            se.wait_ge(s_se.h, nb2)
            dd["s"] = s_se.inc(se.activation(out=rstd_all[:], in_=stat_s[:], func=AF.Exp,
                                             scale=-0.5))
        return lambda: dd["s"]

    def emit_input_stats(s):
        ldn = LD()
        ven = LD()
        slots = [s_xa, s_xb]
        ent = LD()

        @phase("scalar")
        def ld(sy, ldn=ldn, ven=ven, s=s, ent=ent):
            ent["v"] = s_ve.n
            for tc in range(NT):
                sy.wait_ge(s_ve.h, ven[tc - 2] if tc >= 2 else ent["v"])
                ldn[tc] = slots[tc % 2].inc(sy.dma_start(
                    out=rbuf2.rearrange("p (s w) -> p s w", s=2)[:, tc % 2, :],
                    in_=x_in[s, tc * 128:(tc + 1) * 128, :]), 16)

        @phase("vector")
        def st(v, ldn=ldn, ven=ven):
            for tc in range(NT):
                v.wait_ge(slots[tc % 2].h, ldn[tc])
                ven[tc] = s_ve.inc(stats_cols(
                    v, rbuf2.rearrange("p (s w) -> p s w", s=2)[:, tc % 2, :], tc))
        return emit_stats_finish()

    def emit_xhat(rsrc, rstd_ready):
        slots = [s_xa, s_xb]
        ldn = LD()
        affn = LD()
        tpn = LD()
        evn = LD()
        ent = LD()

        def _sev(se, tc):
            se.wait_ge(s_mm.h, tpn[tc])
            srcp = pt if tc % 2 == 0 else pv
            evn[tc] = s_se.inc(se.activation(
                out=xhT.rearrange("p (c n) -> p c n", c=ND)[:, :, tc * 128:(tc + 1) * 128],
                in_=srcp[:].rearrange("p (c f) -> p c f", c=ND), func=AF.Copy))

        @phase("scalar")
        def ld(sy, ldn=ldn, affn=affn, ent=ent):
            ent["v"] = s_ve.n
            ent["oa"], ent["ob"] = s_oa.n, s_ob.n
            sy.wait_ge(s_oa.h, ent["oa"])
            sy.wait_ge(s_ob.h, ent["ob"])
            for tc in range(NT):
                sy.wait_ge(s_ve.h, affn[tc - 2] if tc >= 2 else ent["v"])
                ldn[tc] = slots[tc % 2].inc(sy.dma_start(
                    out=rbuf2.rearrange("p (s w) -> p s w", s=2)[:, tc % 2, :],
                    in_=rsrc[tc * 128:(tc + 1) * 128, :]), 16)
                if tc >= 2:
                    _sev(sy, tc - 2)
            _sev(sy, NT - 2)
            _sev(sy, NT - 1)

        @phase("vector")
        def aff(v, ldn=ldn, affn=affn, tpn=tpn):
            v.wait_ge(s_se.h, rstd_ready())
            for tc in range(NT):
                v.wait_ge(slots[tc % 2].h, ldn[tc])
                if tc >= 2:
                    v.wait_ge(s_mm.h, tpn[tc - 2])
                affn[tc] = s_ve.inc(v.tensor_scalar(
                    out=xbt.rearrange("p (s w) -> p s w", s=2)[:, tc % 2, :],
                    in0=rbuf2.rearrange("p (s w) -> p s w", s=2)[:, tc % 2, :],
                    scalar1=mean_all[:, tc:tc + 1], scalar2=rstd_all[:, tc:tc + 1],
                    op0=ALU.subtract, op1=ALU.mult))

        @phase("tensor")
        def tp(t, affn=affn, tpn=tpn, evn=evn):
            for tc in range(NT):
                t.wait_ge(s_ve.h, affn[tc])
                if tc >= 2:
                    t.wait_ge(s_se.h, evn[tc - 2])
                dstp = pt if tc % 2 == 0 else pv
                xb = xbt.rearrange("p (s w) -> p s w", s=2)[:, tc % 2, :]
                for dc in range(ND):
                    ii = t.matmul(dstp[:, dc * 128:(dc + 1) * 128],
                                  xb.rearrange("p (c f) -> p c f", c=ND)[:, dc, :],
                                  ident[:], start=True, stop=True)
                tpn[tc] = s_mm.inc(ii)
        return lambda: evn[NT - 1]

    def emit_w_cast(dst_sb, src_name):
        src_ap = w_in[src_name]
        Rr = src_ap.shape[0] // 128
        out = LD()

        @phase("gpsimd")
        def g(gp, out=out):
            out["mm"], out["se"], out["ve"] = s_mm.n, s_se.n, s_ve.n
            gp.wait_ge(s_mm.h, out["mm"])
            gp.wait_ge(s_se.h, out["se"])
            gp.wait_ge(s_ve.h, out["ve"])
            out["n"] = s_wg.inc(gp.dma_start(
                out=dst_sb.rearrange("p (c f) -> p c f", c=Rr),
                in_=src_ap.rearrange("(c p) f -> p c f", p=128)), 16)
        return lambda: out["n"]

    def emit_qkv_l0(xh_ready):
        wq_n = emit_w_cast(wq_sb, "l0_wqT")
        wk_n = emit_w_cast(wk_sb, "l0_wkT")
        wv_n = emit_w_cast(wv_sb, "l0_wvT")
        mm_p = LD(); tp_p = LD(); se_rs = LD(); ve_raw = LD(); ve_ms = LD()
        ve_ap = LD(); ve_ev = LD(); ve_vv = LD()
        COLS = {"q": (0, 16), "k": (32, 48)}
        ROFF = {"q": 0, "k": 1024}

        @phase("tensor")
        def tmm(t):
            t.wait_ge(s_se.h, xh_ready())
            t.wait_ge(s_wg.h, wq_n())
            for tc in range(NT):
                xsl = xhT.rearrange("p (c n) -> p c n", c=ND)[:, :, tc * 128:(tc + 1) * 128]
                for which, psd, wsb in (("q", pq, wq_sb), ("k", pk, wk_sb)):
                    if tc == 0 and which == "k":
                        t.wait_ge(s_wg.h, wk_n())
                    if tc >= 1:
                        t.wait_ge(s_ve.h, ve_raw[(which, tc - 1)])
                    for ob in range(2):
                        for dc in range(ND):
                            ii = t.matmul(psd[:, ob * 512:(ob + 1) * 512], xsl[:, dc, :],
                                          wsb.rearrange("p (c f) -> p c f", c=ND)[:, dc,
                                                                                  ob * 512:(ob + 1) * 512],
                                          start=(dc == 0), stop=(dc == ND - 1))
                    mm_p[(which, tc)] = s_mm.inc(ii)
                if tc == 0:
                    t.wait_ge(s_wg.h, wv_n())
                if tc >= 1:
                    t.wait_ge(s_ve.h, ve_vv[tc - 1])
                for ob in range(2):
                    for dc in range(ND):
                        ii = t.matmul(pv[:, ob * 512:(ob + 1) * 512], xsl[:, dc, :],
                                      wv_sb.rearrange("p (c f) -> p c f", c=ND)[:, dc,
                                                                                ob * 512:(ob + 1) * 512],
                                      start=(dc == 0), stop=(dc == ND - 1))
                mm_p[("v", tc)] = s_mm.inc(ii)
                if tc >= 1:
                    _tp(t, tc - 1)
            _tp(t, NT - 1)

        def _tp(t, tc):
            t.wait_ge(s_ve.h, ve_ap[("q", tc)])
            if tc >= 1:
                t.wait_ge(s_ve.h, ve_ev[("k", tc - 1)])
            for dc in range(ND):
                ii = t.matmul(pt[:, dc * 128:(dc + 1) * 128],
                              xbt[:, dc * 128:(dc + 1) * 128], ident[:],
                              start=True, stop=True)
            tp_p[("q", tc)] = s_mm.inc(ii)
            t.wait_ge(s_ve.h, ve_ap[("k", tc)])
            t.wait_ge(s_ve.h, ve_ev[("q", tc)])
            for dc in range(ND):
                ii = t.matmul(pt[:, dc * 128:(dc + 1) * 128],
                              xbt[:, 1024 + dc * 128:1024 + (dc + 1) * 128], ident[:],
                              start=True, stop=True)
            tp_p[("k", tc)] = s_mm.inc(ii)

        @phase("scalar")
        def tse(se):
            for tc in range(NT):
                for which in ("q", "k"):
                    c0, c1 = COLS[which]
                    se.wait_ge(s_ve.h, ve_ms[(which, tc)])
                    nb2 = s_se.inc(se.activation(out=rmsbuf[:, c0 + 16:c1 + 16],
                                                 in_=rmsbuf[:, c0:c1],
                                                 func=AF.Ln, bias=eps_rms[:], scale=1.0 / DH))
                    se.wait_ge(s_se.h, nb2)
                    se_rs[(which, tc)] = s_se.inc(
                        se.activation(out=rmsbuf[:, c0:c1], in_=rmsbuf[:, c0 + 16:c1 + 16],
                                      func=AF.Exp, scale=-0.5))

        @phase("vector")
        def tve(v):
            for tc in range(NT):
                for which, psd, wbc, xoff in (("q", pq, wqbc, 0), ("k", pk, wkbc, 1024)):
                    c0, c1 = COLS[which]
                    raw = robuf.rearrange("p (s w) -> p s w", s=2)[:, ROFF[which] // 1024, :]
                    v.wait_ge(s_mm.h, mm_p[(which, tc)])
                    ve_raw[(which, tc)] = s_ve.inc(v.tensor_copy(out=raw, in_=psd[:]))
                    v.tensor_tensor(out=sqt[:], in0=raw, in1=raw, op=ALU.mult)
                    ve_ms[(which, tc)] = s_ve.inc(v.tensor_reduce(
                        out=rmsbuf[:, c0:c1], in_=sqt[:].rearrange("p (h d) -> p h d", h=H),
                        axis=AX.X, op=ALU.add))
                    v.wait_ge(s_se.h, se_rs[(which, tc)])
                    rexp = rmsbuf[:, c0:c1].rearrange("p (h o) -> p h o", o=1).broadcast_to([128, H, DH])
                    v.tensor_tensor(out=sqt[:].rearrange("p (h d) -> p h d", h=H),
                                    in0=raw.rearrange("p (h d) -> p h d", h=H),
                                    in1=rexp, op=ALU.mult)
                    v.wait_ge(s_mm.h, tp_p[(which, tc - 1)] if tc >= 1 else 0)
                    wexp = wbc[:].rearrange("p (o d) -> p o d", o=1).broadcast_to([128, H, DH])
                    ve_ap[(which, tc)] = s_ve.inc(v.tensor_tensor(
                        out=xbt[:, xoff:xoff + 1024].rearrange("p (h d) -> p h d", h=H),
                        in0=sqt[:].rearrange("p (h d) -> p h d", h=H), in1=wexp, op=ALU.mult))
                    if tc >= 1:
                        ti = 0 if which == "q" else 8
                        ve_ev[(which, tc - 1)] = s_ve.inc(v.tensor_copy(
                            out=qkhT.rearrange("p (c n) -> p c n", c=16)[:, ti:ti + 8,
                                                                        (tc - 1) * 128:tc * 128],
                            in_=pt[:].rearrange("p (c f) -> p c f", c=ND)))
                v.wait_ge(s_mm.h, mm_p[("v", tc)])
                v.tensor_copy(
                    out=v_t.rearrange("p (t h c) -> p t h c", h=H, c=VP)[:, tc, :, 0:DH],
                    in_=pv[:].rearrange("p (h d) -> p h d", h=H))
                ve_vv[tc] = s_ve.inc(v.memset(
                    v_t.rearrange("p (t h c) -> p t h c", h=H, c=VP)[:, tc, :, DH:VP], 1.0))
            for which in ("q", "k"):
                v.wait_ge(s_mm.h, tp_p[(which, NT - 1)])
                ti = 0 if which == "q" else 8
                ve_ev[(which, NT - 1)] = s_ve.inc(v.tensor_copy(
                    out=qkhT.rearrange("p (c n) -> p c n", c=16)[:, ti:ti + 8,
                                                                (NT - 1) * 128:NT * 128],
                    in_=pt[:].rearrange("p (c f) -> p c f", c=ND)))
        return lambda: (s_ve.n, s_mm.n)

    def emit_qkv_l1(xh_ready):
        v_n = emit_w_cast(l1v_sb, "l1_vT")
        u_n = emit_w_cast(l1u_sb, "l1_uT")
        mm_t1 = LD(); ve_t1 = LD(); mm_qk = LD(); ve_qk = LD(); se_sq = LD(); mm_nr = LD(); ve_nr = LD()
        mm_v2 = LD(); ve_v2 = LD()

        @phase("tensor")
        def tmm(t):
            t.wait_ge(s_se.h, xh_ready())
            t.wait_ge(s_wg.h, u_n())
            for nb in range(NQB):
                if nb >= 1:
                    t.wait_ge(s_ve.h, ve_t1[nb - 1])
                for m2 in range(2):
                    for dc in range(ND):
                        ii = t.matmul(pq[:, m2 * 512:(m2 + 1) * 512],
                                      l1v_sb.rearrange("p (c f) -> p c f", c=ND)[:, dc,
                                                                                 m2 * 128:(m2 + 1) * 128],
                                      xhT.rearrange("p (c n) -> p c n", c=ND)[:, dc,
                                                                              nb * 512:(nb + 1) * 512],
                                      start=(dc == 0), stop=(dc == ND - 1))
                mm_t1[nb] = s_mm.inc(ii)
                t.wait_ge(s_ve.h, ve_t1[nb])
                for od in range(16):
                    if od >= 2:
                        t.wait_ge(s_ve.h, ve_qk[(nb, od - 2)])
                    elif nb >= 1:
                        t.wait_ge(s_ve.h, ve_qk[(nb - 1, od + 14)])
                    for kc in range(2):
                        ii = t.matmul(pk[:, (od % 2) * 512:(od % 2) * 512 + 512],
                                      l1u_sb.rearrange("p (c f) -> p c f", c=2)[:, kc,
                                                                                od * 128:(od + 1) * 128],
                                      tsb.rearrange("p (s w) -> p s w", s=2)[:, kc, :],
                                      start=(kc == 0), stop=(kc == 1))
                    mm_qk[(nb, od)] = s_mm.inc(ii)
                    t.wait_ge(s_se.h, se_sq[(nb, od)])
                    if od >= 1:
                        t.wait_ge(s_ve.h, ve_nr[(nb, od - 1)])
                    elif nb >= 1:
                        t.wait_ge(s_ve.h, ve_nr[(nb - 1, 15)])
                    t.matmul(pt[0:1, 0:512], bd2[:, 0:1], xbt[:, 0:512],
                             start=True, stop=True)
                    mm_nr[(nb, od)] = s_mm.inc(t.matmul(
                        pt[0:1, 512:1024], bd2[:, 1:2], xbt[:, 0:512],
                        start=True, stop=True))
                for tci in range(4):
                    if tci >= 1:
                        t.wait_ge(s_ve.h, ve_v2[(nb, tci - 1)])
                    elif nb >= 1:
                        t.wait_ge(s_ve.h, ve_v2[(nb - 1, 3)])
                    for vb in range(2):
                        for kc in range(2):
                            ii = t.matmul(pv[:, vb * 512:(vb + 1) * 512],
                                          tsb.rearrange("p (s w) -> p s w", s=2)[:, kc,
                                                                                 tci * 128:(tci + 1) * 128],
                                          l1u_sb.rearrange("p (c f) -> p c f", c=2)[:, kc,
                                                                                    2048 + vb * 512:2048 + (vb + 1) * 512],
                                          start=(kc == 0), stop=(kc == 1))
                    mm_v2[(nb, tci)] = s_mm.inc(ii)

        @phase("scalar")
        def tse(se):
            for nb in range(NQB):
                for od in range(16):
                    se.wait_ge(s_ve.h, ve_qk[(nb, od)])
                    se_sq[(nb, od)] = s_se.inc(se.activation(
                        out=xbt[:, 0:512],
                        in_=qkhT.rearrange("p (c n) -> p c n", c=16)[:, od,
                                                                    nb * 512:(nb + 1) * 512],
                        func=AF.Square))

        @phase("vector")
        def tve(v):
            for nb in range(NQB):
                v.wait_ge(s_mm.h, mm_t1[nb])
                ve_t1[nb] = s_ve.inc(v.tensor_copy(out=tsb[:], in_=pq[:]))
                for od in range(16):
                    v.wait_ge(s_mm.h, mm_qk[(nb, od)])
                    ve_qk[(nb, od)] = s_ve.inc(v.tensor_copy(
                        out=qkhT.rearrange("p (c n) -> p c n", c=16)[:, od,
                                                                    nb * 512:(nb + 1) * 512],
                        in_=pk[:, (od % 2) * 512:(od % 2) * 512 + 512]))
                    v.wait_ge(s_mm.h, mm_nr[(nb, od)])
                    qk = 0 if od < 8 else 1
                    j = od % 8
                    c0 = qk * 64 + (2 * j) * 4 + nb
                    c1 = qk * 64 + (2 * j + 1) * 4 + nb
                    v.tensor_reduce(out=normrow[0:1, c0:c0 + 1], in_=pt[0:1, 0:512],
                                    axis=AX.X, op=ALU.max)
                    ve_nr[(nb, od)] = s_ve.inc(v.tensor_reduce(
                        out=normrow[0:1, c1:c1 + 1], in_=pt[0:1, 512:1024],
                        axis=AX.X, op=ALU.max))
                for tci in range(4):
                    v.wait_ge(s_mm.h, mm_v2[(nb, tci)])
                    v.tensor_copy(
                        out=v_t.rearrange("p (t h c) -> p t h c", h=H, c=VP)[:, nb * 4 + tci, :, 0:DH],
                        in_=pv[:].rearrange("p (h d) -> p h d", h=H))
                    ve_v2[(nb, tci)] = s_ve.inc(v.memset(
                        v_t.rearrange("p (t h c) -> p t h c", h=H, c=VP)[:, nb * 4 + tci, :, DH:VP], 1.0))

        dd = LD()

        @phase("vector")
        def fin1(v, dd=dd):
            v.wait_ge(s_ve.h, ve_nr[(NQB - 1, 15)])
            v.tensor_reduce(out=normrow[0:1, 128:144],
                            in_=normrow[0:1, 0:64].rearrange("p (h n) -> p h n", h=16),
                            axis=AX.X, op=ALU.max)
            dd["v0"] = s_ve.inc(v.tensor_reduce(
                out=normrow[0:1, 144:160],
                in_=normrow[0:1, 64:128].rearrange("p (h n) -> p h n", h=16),
                axis=AX.X, op=ALU.max))

        @phase("scalar")
        def fin2(se, dd=dd):
            se.wait_ge(s_ve.h, dd["v0"])
            se.activation(out=normrow[0:1, 128:144], in_=normrow[0:1, 128:144],
                          func=AF.Ln, bias=eps_rms[0:1, :]) if False else None
            se.activation(out=normrow[0:1, 160:176], in_=normrow[0:1, 128:144],
                          func=AF.Ln, bias=eps_rms[0:1, :])
            dd["s0"] = s_se.inc(se.activation(out=normrow[0:1, 128:144],
                                              in_=normrow[0:1, 144:160],
                                              func=AF.Ln, bias=eps_rms[0:1, :]))

        @phase("vector")
        def fin3(v, dd=dd):
            v.wait_ge(s_se.h, dd["s0"])
            dd["v1"] = s_ve.inc(v.tensor_add(out=normrow[0:1, 144:160],
                                             in0=normrow[0:1, 160:176],
                                             in1=normrow[0:1, 128:144]))

        @phase("scalar")
        def fin4(se, dd=dd):
            se.wait_ge(s_ve.h, dd["v1"])
            dd["s1"] = s_se.inc(se.activation(out=normrow[0:1, 160:176],
                                              in_=normrow[0:1, 144:160],
                                              func=AF.Exp, scale=0.5))

        @phase("vector")
        def fin5(v, dd=dd):
            v.wait_ge(s_se.h, dd["s1"])
            v.tensor_scalar(out=normrow[0:1, 128:144], in0=normrow[0:1, 160:176],
                            scalar1=-0.125, scalar2=0.0, op0=ALU.mult, op1=ALU.add)
            dd["v2"] = s_ve.inc(v.tensor_copy(out=crow_b[:], in_=normrow[0:1, 128:144]))

        @phase("tensor")
        def fin6(t, dd=dd):
            t.wait_ge(s_ve.h, dd["v2"])
            dd["m1"] = s_mm.inc(t.matmul(pt[0:128, 800:816], onesr_b[:], crow_b[:],
                                         start=True, stop=True))

        @phase("vector")
        def fin9(v, dd=dd):
            v.wait_ge(s_mm.h, dd["m1"])
            dd["v4"] = s_ve.inc(v.tensor_copy(out=bias_l1[:], in_=pt[0:128, 800:816]))
        return lambda: (dd["v4"], s_mm.n)

    def emit_attn_out(l, ready, rsrc, rdst, fold_stats):
        if l == 0:
            wo_n = emit_w_cast(wo_sb, "l0_woT")
        else:
            ov_n = emit_w_cast(l1ov_sb, "l1_ovT")
            ou_n = emit_w_cast(l1ou_sb, "l1_ouT")
        mm_sc = LD(); se_pe = LD(); mm_av = LD(); mm_bc = LD()
        ve_rc = LD(); ve_eb = LD(); ve_no = LD()
        ldn = LD(); mm_o = LD(); ve_ro = LD(); stn = LD(); mm_t2 = LD(); ve_t2 = LD()
        slots = [s_xa, s_xb]
        oslots = [s_oa, s_ob]
        ent = LD()

        def qsl(h, cols):
            return qkhT.rearrange("p (c n) -> p c n", c=16)[(h % 2) * 64:(h % 2) * 64 + 64,
                                                            h // 2, cols]

        def ksl(h, cols):
            return qkhT.rearrange("p (c n) -> p c n", c=16)[(h % 2) * 64:(h % 2) * 64 + 64,
                                                            8 + h // 2, cols]

        @phase("sync")
        def ld(sy):
            ent["v"] = s_ve.n
            ent["oa"], ent["ob"] = s_oa.n, s_ob.n
            sy.wait_ge(s_oa.h, ent["oa"])
            sy.wait_ge(s_ob.h, ent["ob"])
            for tc in range(NT):
                sy.wait_ge(s_ve.h, ve_ro[tc - 2] if tc >= 2 else ent["v"])
                ldn[tc] = slots[tc % 2].inc(sy.dma_start(
                    out=rbuf.rearrange("p (s w) -> p s w", s=2)[:, tc % 2, :],
                    in_=rsrc[tc * 128:(tc + 1) * 128, :]), 16)
                if tc >= 2:
                    _st(sy, tc - 2)
            _st(sy, NT - 2)
            _st(sy, NT - 1)

        def _st(sy, tc):
            sy.wait_ge(s_ve.h, ve_ro[tc])
            stn[tc] = oslots[tc % 2].inc(sy.dma_start(
                out=rdst[tc * 128:(tc + 1) * 128, :],
                in_=robuf.rearrange("p (s w) -> p s w", s=2)[:, tc % 2, :]), 16)

        def proj_pe(t, cq, h):
            qbp = cq - 1
            if l == 0:
                if h % 2 == 1:
                    return
                j = h // 2
                tc = qbp * 4 + j // 2
                ob = j % 2
                if h == 0:
                    t.wait_ge(s_ve.h, ve_no[qbp * H + H - 1])
                if ob == 0 and tc >= 1:
                    t.wait_ge(s_ve.h, ve_ro[tc - 1])
                for dc in range(ND):
                    ii = t.matmul(pt[:, ob * 512:(ob + 1) * 512],
                                  xhT.rearrange("p (c n) -> p c n", c=ND)[:, dc,
                                                                          tc * 128:(tc + 1) * 128],
                                  wo_sb.rearrange("p (c f) -> p c f", c=ND)[:, dc,
                                                                            ob * 512:(ob + 1) * 512],
                                  start=(dc == 0), stop=(dc == ND - 1))
                if ob == 1:
                    mm_o[tc] = s_mm.inc(ii)
            else:
                if h in (0, 2):
                    m2 = h // 2
                    if h == 0:
                        t.wait_ge(s_ve.h, ve_no[qbp * H + H - 1])
                        if qbp * 4 - 1 >= 0:
                            t.wait_ge(s_ve.h, ve_ro[qbp * 4 - 1])
                    for dc in range(ND):
                        ii = t.matmul(pt[:, m2 * 512:(m2 + 1) * 512],
                                      l1ov_sb.rearrange("p (c f) -> p c f", c=ND)[:, dc,
                                                                                  m2 * 128:(m2 + 1) * 128],
                                      xhT.rearrange("p (c n) -> p c n", c=ND)[:, dc,
                                                                              qbp * 512:(qbp + 1) * 512],
                                      start=(dc == 0), stop=(dc == ND - 1))
                    if h == 2:
                        mm_t2[qbp] = s_mm.inc(ii)
                elif 4 <= h < 12:
                    u = h - 4
                    tci = u // 2
                    tc = qbp * 4 + tci
                    ob = u % 2
                    if h == 4:
                        t.wait_ge(s_ve.h, ve_t2[qbp])
                    if ob == 0 and tci >= 1:
                        t.wait_ge(s_ve.h, ve_ro[tc - 1])
                    for kc in range(2):
                        ii = t.matmul(pt[:, ob * 512:(ob + 1) * 512],
                                      t2sb.rearrange("p (s w) -> p s w", s=2)[:, kc,
                                                                              tci * 128:(tci + 1) * 128],
                                      l1ou_sb.rearrange("p (c f) -> p c f", c=2)[:, kc,
                                                                                 ob * 512:(ob + 1) * 512],
                                      start=(kc == 0), stop=(kc == 1))
                    if ob == 1:
                        mm_o[tc] = s_mm.inc(ii)

        def proj_ve(v, cq, h):
            qbp = cq - 1
            if l == 1 and h == 2:
                v.wait_ge(s_mm.h, mm_t2[qbp])
                if qbp >= 1:
                    v.wait_ge(s_mm.h, mm_o[qbp * 4 - 1])
                ve_t2[qbp] = s_ve.inc(v.tensor_copy(out=t2sb[:], in_=pt[:]))
                return
            if l == 0:
                if h % 4 != 2:
                    return
                tci = h // 4
            else:
                if h not in (5, 7, 9, 11):
                    return
                tci = (h - 5) // 2
            tc = qbp * 4 + tci
            v.wait_ge(s_mm.h, mm_o[tc])
            v.wait_ge(slots[tc % 2].h, ldn[tc])
            if tc >= 2:
                v.wait_ge(oslots[tc % 2].h, stn[tc - 2])
            ro = robuf.rearrange("p (s w) -> p s w", s=2)[:, tc % 2, :]
            ii = v.tensor_add(out=ro, in0=pt[:],
                              in1=rbuf.rearrange("p (s w) -> p s w", s=2)[:, tc % 2, :])
            if fold_stats:
                ii = stats_cols(v, ro, tc)
            ve_ro[tc] = s_ve.inc(ii)

        def emit_bcast(t, bi):
            t.wait_ge(s_ve.h, ve_rc[bi])
            mm_bc[bi] = s_mm.inc(t.matmul(pk[0:64, 512:512 + QB], ones64[:],
                                          rrow[0:1, 0:QB], start=True, stop=True))

        @phase("tensor")
        def tmm(t):
            vn, mn = ready()
            t.wait_ge(s_ve.h, vn)
            if l == 0:
                t.wait_ge(s_wg.h, wo_n())
            else:
                t.wait_ge(s_wg.h, ou_n())
            for cq in range(NQB + 1):
                cols = slice(cq * QB, (cq + 1) * QB)
                for h in range(H):
                    if cq < NQB:
                        idx = cq * H + h
                        for kc in range(NT):
                            gi = idx * NT + kc
                            if gi >= 4:
                                t.wait_ge(s_se.h, se_pe[(gi - 4) // 2])
                            psd = pq if kc % 4 < 2 else pv
                            mm_sc[gi] = s_mm.inc(t.matmul(
                                psd[:, (kc % 2) * 512:(kc % 2) * 512 + 512],
                                ksl(h, slice(kc * 128, (kc + 1) * 128)), qsl(h, cols),
                                start=True, stop=True))
                            if kc == 3 and idx >= 1:
                                emit_bcast(t, idx - 1)
                    elif h == 0:
                        emit_bcast(t, NQB * H - 1)
                    if cq >= 1:
                        proj_pe(t, cq, h)
                    if cq == NQB:
                        continue
                    if idx >= 1:
                        t.wait_ge(s_ve.h, ve_no[idx - 1])
                    for kc in range(NT):
                        t.wait_ge(s_se.h, se_pe[idx * 8 + kc // 2])
                        ii = t.matmul(pk[0:VP, 0:QB],
                                      v_t.rearrange("p (t h c) -> p t h c", h=H, c=VP)[:, kc, h, :],
                                      PT.rearrange("p (t w) -> p t w", t=NT)[:, kc, :],
                                      start=(kc == 0), stop=(kc == NT - 1))
                    mm_av[idx] = s_mm.inc(ii)

        @phase("scalar")
        def tse(se):
            for qb in range(NQB):
                for h in range(H):
                    idx = qb * H + h
                    if idx >= 1:
                        se.wait_ge(s_mm.h, mm_av[idx - 1])
                    for j in range(8):
                        se.wait_ge(s_mm.h, mm_sc[idx * NT + 2 * j + 1])
                        bias = bias_l0[:] if l == 0 else bias_l1[:, h:h + 1]
                        src = pq if j % 2 == 0 else pv
                        se_pe[idx * 8 + j] = s_se.inc(se.activation(
                            out=PT[:, j * 1024:(j + 1) * 1024],
                            in_=src[:, 0:1024],
                            func=AF.Exp, bias=bias, scale=0.125))

        @phase("vector")
        def tve(v):
            for cq in range(NQB + 1):
                for h in range(H):
                    if cq < NQB:
                        idx = cq * H + h
                        v.wait_ge(s_mm.h, mm_av[idx])
                        with nc.allow_low_precision(reason="bf16 denominator reciprocal"):
                            nrc = s_ve.inc(v.reciprocal(out=rrow[0:1, 0:QB], in_=pk[64:65, 0:QB]))
                        v.wait_ge(s_ve.h, nrc)
                        ve_rc[idx] = s_ve.inc(v.tensor_scalar(
                            out=qcol[:, 11:12], in0=bias_l0[:], scalar1=0.0,
                            scalar2=0.0, op0=ALU.mult, op1=ALU.add))
                        v.wait_ge(s_mm.h, mm_bc[idx])
                        ve_eb[idx] = s_ve.inc(v.tensor_copy(out=rb_sb[:], in_=pk[0:64, 512:512 + QB]))
                        dst = xhT.rearrange("p (c n) -> p c n", c=ND)[(h % 2) * 64:(h % 2) * 64 + 64,
                                                                      h // 2, cq * QB:(cq + 1) * QB]
                        ve_no[idx] = s_ve.inc(v.tensor_tensor(out=dst, in0=pk[0:64, 0:QB],
                                                              in1=rb_sb[:], op=ALU.mult))
                    if cq >= 1:
                        proj_ve(v, cq, h)
        fin = emit_stats_finish() if fold_stats else (lambda: s_se.n)
        return fin

    def emit_ffn(l, rstd_ready, rsrc, rdst, fold_stats):
        slots = [s_xa, s_xb]
        oslots = [s_oa, s_ob]
        gsc, usc, dsc = q8[(l, "g")], q8[(l, "u")], q8[(l, "d")]
        prev = {}
        for cc in range(4):
            base_tc = cc * 4
            ldn = LD(); affn = LD(); tpn = LD(); evn = LD()
            ent = LD(); ent_cc = LD()

            @phase("sync")
            def lda(sy, ldn=ldn, affn=affn, base_tc=base_tc, ent=ent, ent_cc=ent_cc):
                ent_cc["mm"], ent_cc["ve"] = s_mm.n, s_ve.n
                ent["v"] = s_ve.n
                ent["oa"], ent["ob"] = s_oa.n, s_ob.n
                sy.wait_ge(s_oa.h, ent["oa"])
                sy.wait_ge(s_ob.h, ent["ob"])
                for i in range(4):
                    tc = base_tc + i
                    sy.wait_ge(s_ve.h, affn[i - 2] if i >= 2 else ent["v"])
                    ldn[i] = slots[i % 2].inc(sy.dma_start(
                        out=rbuf.rearrange("p (s w) -> p s w", s=2)[:, i % 2, :],
                        in_=rsrc[tc * 128:(tc + 1) * 128, :]), 16)

            @phase("vector")
            def ffa(v, ldn=ldn, affn=affn, tpn=tpn, base_tc=base_tc, cc=cc, l=l):
                if cc == 0:
                    v.wait_ge(s_se.h, rstd_ready())
                for i in range(4):
                    tc = base_tc + i
                    v.wait_ge(slots[i % 2].h, ldn[i])
                    if i >= 2:
                        v.wait_ge(s_mm.h, tpn[i - 2])
                    xb = xbt.rearrange("p (s w) -> p s w", s=2)[:, i % 2, :]
                    v.tensor_scalar(out=xb,
                                    in0=rbuf.rearrange("p (s w) -> p s w", s=2)[:, i % 2, :],
                                    scalar1=mean_all[:, tc:tc + 1], scalar2=rstd_all[:, tc:tc + 1],
                                    op0=ALU.subtract, op1=ALU.mult)
                    nbar = s_ve.inc(v.tensor_reduce(out=qcol[:, 0:1], in_=xb, axis=AX.X,
                                                    op=ALU.max, apply_absolute_value=True))
                    v.wait_ge(s_ve.h, nbar)
                    v.tensor_scalar(out=qcol[:, 1:2], in0=qcol[:, 0:1], scalar1=1e-5,
                                    scalar2=0.0, op0=ALU.max, op1=ALU.add)
                    nbar = s_ve.inc(v.reciprocal(out=qcol[:, 2:3], in_=qcol[:, 1:2]))
                    v.wait_ge(s_ve.h, nbar)
                    v.tensor_scalar(out=qcol[:, 3:4], in0=qcol[:, 2:3], scalar1=127.0,
                                    scalar2=0.0, op0=ALU.mult, op1=ALU.add)
                    v.tensor_scalar(out=fmisc[:, 0:1024], in0=xb, scalar1=qcol[:, 3:4],
                                    scalar2=CRND, op0=ALU.mult, op1=ALU.add)
                    v.tensor_scalar(out=smisc[:, 8 + i:9 + i], in0=qcol[:, 1:2],
                                    scalar1=1.0 / 127.0, scalar2=0.0, op0=ALU.mult, op1=ALU.add)
                    v.tensor_tensor(out=smisc[:, 0 + i:1 + i], in0=smisc[:, 8 + i:9 + i],
                                    in1=swmeans[:, 3 * l:3 * l + 1], op=ALU.mult)
                    v.tensor_tensor(out=smisc[:, 4 + i:5 + i], in0=smisc[:, 8 + i:9 + i],
                                    in1=swmeans[:, 3 * l + 1:3 * l + 2], op=ALU.mult)
                    affn[i] = s_ve.inc(v.tensor_scalar(out=xb, in0=fmisc[:, 0:1024],
                                                       scalar1=CRND, scalar2=0.0,
                                                       op0=ALU.subtract, op1=ALU.add))

            @phase("tensor")
            def fft(t, affn=affn, tpn=tpn, evn=evn):
                for i in range(4):
                    t.wait_ge(s_ve.h, affn[i])
                    if i >= 2:
                        t.wait_ge(s_se.h, evn[i - 2])
                    dstp = pt if i % 2 == 0 else pv
                    xb = xbt.rearrange("p (s w) -> p s w", s=2)[:, i % 2, :]
                    for dc in range(ND):
                        ii = t.matmul(dstp[:, dc * 128:(dc + 1) * 128],
                                      xb.rearrange("p (c f) -> p c f", c=ND)[:, dc, :],
                                      ident[:], start=True, stop=True)
                    tpn[i] = s_mm.inc(ii)

            @phase("scalar")
            def ffe(se, tpn=tpn, evn=evn):
                for i in range(4):
                    se.wait_ge(s_mm.h, tpn[i])
                    srcp = pt if i % 2 == 0 else pv
                    evn[i] = s_se.inc(se.activation(
                        out=xiT.rearrange("p (c n) -> p c n", c=ND)[:, :, i * 128:(i + 1) * 128],
                        in_=srcp[:].rearrange("p (c f) -> p c f", c=ND), func=AF.Copy))

            gld = LD(); mm_g = LD(); mm_u = LD(); se_sil = LD(); ve_h = LD()
            dld = LD(); mm_dq = LD()

            def wtrig(se, ib, gld=gld):
                s_wl.inc(se.dma_start(
                    out=wg_sb.rearrange("p (s c f) -> p s c f", s=2, c=8)[:, ib % 2, :, :],
                    in_=gsc[ib, :, :, :].rearrange("c p f -> p c f")), 16)
                gld[ib] = s_wl.inc(se.dma_start(
                    out=wu_sb.rearrange("p (s c f) -> p s c f", s=2, c=8)[:, ib % 2, :, :],
                    in_=usc[ib, :, :, :].rearrange("c p f -> p c f")), 16)

            @phase("tensor")
            def gmm(t, gld=gld, mm_g=mm_g, mm_u=mm_u, ve_h=ve_h, evn=evn):
                t.wait_ge(s_se.h, evn[3])
                for ib in range(8):
                    t.wait_ge(s_wl.h, gld[ib])
                    for tci in range(4):
                        gi = ib * 4 + tci
                        if gi >= 2:
                            t.wait_ge(s_ve.h, ve_h[((gi - 2) // 4, (gi - 2) % 4)])
                        bank = gi % 2
                        for psd, wsb, dd_ in ((pq, wg_sb, mm_g), (pk, wu_sb, mm_u)):
                            for dc in range(ND):
                                ii = t.matmul(psd[:, bank * 512:(bank + 1) * 512],
                                              xiT.rearrange("p (c n) -> p c n", c=ND)[:, dc,
                                                                                      tci * 128:(tci + 1) * 128],
                                              wsb.rearrange("p (s c f) -> p s c f", s=2, c=8)[:, ib % 2,
                                                                                              dc, :],
                                              start=(dc == 0), stop=(dc == ND - 1))
                            dd_[(ib, tci)] = s_mm.inc(ii)

            @phase("scalar")
            def gse(se, mm_g=mm_g, mm_u=mm_u, se_sil=se_sil, ve_h=ve_h, gld=gld,
                    dld=dld, mm_dq=mm_dq, ent_cc=ent_cc, wtrig=wtrig,
                    pu=prev.get("u"), pd=prev.get("d")):
                for ib in (0, 1):
                    if pu is None:
                        se.wait_ge(s_mm.h, ent_cc["mm"])
                        se.wait_ge(s_ve.h, ent_cc["ve"])
                    else:
                        se.wait_ge(s_mm.h, pu[(6 + ib, 3)])
                    wtrig(se, ib)
                for q in range(2):
                    if pd is None:
                        se.wait_ge(s_mm.h, ent_cc["mm"])
                        se.wait_ge(s_ve.h, ent_cc["ve"])
                    else:
                        se.wait_ge(s_mm.h, pd[(1, q + 2)])
                    dld[(0, q)] = s_wl.inc(se.dma_start(
                        out=wd_sb.rearrange("p (s c f) -> p s c f", s=2, c=8)[:, q % 2, :, :],
                        in_=dsc[0, q * 8:(q + 1) * 8, :, :].rearrange("c p f -> p c f")), 16)
                for ib in range(8):
                    for tci in range(4):
                        gi = ib * 4 + tci
                        se.wait_ge(s_mm.h, mm_g[(ib, tci)])
                        if gi >= 2:
                            se.wait_ge(s_ve.h, ve_h[((gi - 2) // 4, (gi - 2) % 4)])
                        se_sil[(ib, tci)] = s_se.inc(se.activation(
                            out=silt.rearrange("p (s w) -> p s w", s=2)[:, gi % 2, :],
                            in_=pq[:, (gi % 2) * 512:(gi % 2) * 512 + 512], func=AF.Silu,
                            scale=smisc[:, tci:tci + 1]))
                    if ib < 6:
                        se.wait_ge(s_mm.h, mm_u[(ib, 3)])
                        wtrig(se, ib + 2)

            @phase("vector")
            def gve(v, mm_u=mm_u, se_sil=se_sil, ve_h=ve_h):
                for ib in range(8):
                    for tci in range(4):
                        gi = ib * 4 + tci
                        v.wait_ge(s_se.h, se_sil[(ib, tci)])
                        v.wait_ge(s_mm.h, mm_u[(ib, tci)])
                        ve_h[(ib, tci)] = s_ve.inc(v.scalar_tensor_tensor(
                            out=h_t.rearrange("p (t w) -> p t w", t=4)[:, tci,
                                                                       ib * 512:(ib + 1) * 512],
                            in0=pk[:, (gi % 2) * 512:(gi % 2) * 512 + 512],
                            scalar=smisc[:, 4 + tci:5 + tci],
                            in1=silt.rearrange("p (s w) -> p s w", s=2)[:, gi % 2, :],
                            op0=ALU.mult, op1=ALU.mult))

            q2n = LD(); tp2n = LD(); ev2n = LD()

            @phase("vector")
            def hq(v, q2n=q2n, ve_h=ve_h, tp2n=tp2n, l=l):
                for tci in range(4):
                    v.wait_ge(s_ve.h, ve_h[(7, tci)])
                    hsl = h_t.rearrange("p (t w) -> p t w", t=4)[:, tci, :]
                    nbar = s_ve.inc(v.tensor_reduce(out=qcol[:, 4:5], in_=hsl, axis=AX.X,
                                                    op=ALU.max, apply_absolute_value=True))
                    v.wait_ge(s_ve.h, nbar)
                    v.tensor_scalar(out=qcol[:, 5:6], in0=qcol[:, 4:5], scalar1=1e-5,
                                    scalar2=0.0, op0=ALU.max, op1=ALU.add)
                    nbar = s_ve.inc(v.reciprocal(out=qcol[:, 6:7], in_=qcol[:, 5:6]))
                    v.wait_ge(s_ve.h, nbar)
                    v.tensor_scalar(out=qcol[:, 7:8], in0=qcol[:, 6:7], scalar1=127.0,
                                    scalar2=0.0, op0=ALU.mult, op1=ALU.add)
                    v.tensor_scalar(out=qcol[:, 10:11], in0=qcol[:, 5:6],
                                    scalar1=1.0 / 127.0, scalar2=0.0, op0=ALU.mult, op1=ALU.add)
                    v.tensor_tensor(out=smisc[:, 12 + tci:13 + tci], in0=qcol[:, 10:11],
                                    in1=swmeans[:, 3 * l + 2:3 * l + 3], op=ALU.mult)
                    for hh in range(2):
                        v.tensor_scalar(out=fmisc[:], in0=hsl[:, hh * 2048:(hh + 1) * 2048],
                                        scalar1=qcol[:, 7:8], scalar2=CRND,
                                        op0=ALU.mult, op1=ALU.add)
                        ii = v.tensor_scalar(
                            out=x2bf.rearrange("p (t w) -> p t w", t=4)[:, tci,
                                                                        hh * 2048:(hh + 1) * 2048],
                            in0=fmisc[:], scalar1=CRND, scalar2=0.0,
                            op0=ALU.subtract, op1=ALU.add)
                    q2n[tci] = s_ve.inc(ii)

            @phase("tensor")
            def ht(t, q2n=q2n, tp2n=tp2n, ev2n=ev2n):
                for tci in range(4):
                    t.wait_ge(s_ve.h, q2n[tci])
                    for blkc in range(4):
                        gi = tci * 4 + blkc
                        if gi >= 2:
                            t.wait_ge(s_se.h, ev2n[gi - 2])
                        dstp = pt if gi % 2 == 0 else pv
                        for j in range(8):
                            ic = blkc * 8 + j
                            ii = t.matmul(dstp[:, j * 128:(j + 1) * 128],
                                          x2bf.rearrange("p (t w) -> p t w", t=4)[:, tci,
                                                                                  ic * 128:(ic + 1) * 128],
                                          ident[:], start=True, stop=True)
                        tp2n[gi] = s_mm.inc(ii)

            @phase("scalar")
            def hev(se, tp2n=tp2n, ev2n=ev2n):
                for tci in range(4):
                    for blkc in range(4):
                        gi = tci * 4 + blkc
                        se.wait_ge(s_mm.h, tp2n[gi])
                        srcp = pt if gi % 2 == 0 else pv
                        ev2n[gi] = s_se.inc(se.activation(
                            out=x2T.rearrange("p (c n) -> p c n", c=32)[:, blkc * 8:blkc * 8 + 8,
                                                                        tci * 128:(tci + 1) * 128],
                            in_=srcp[:].rearrange("p (c f) -> p c f", c=8), func=AF.Copy))

            @phase("scalar")
            def ldd_se(se, dld=dld, mm_dq=mm_dq):
                for jb, q, gate in ((0, 2, (0, 0)), (0, 3, (0, 1)), (1, 0, (0, 2)),
                                    (1, 1, (0, 3)), (1, 2, (1, 0)), (1, 3, (1, 1))):
                    se.wait_ge(s_mm.h, mm_dq[gate])
                    dld[(jb, q)] = s_wl.inc(se.dma_start(
                        out=wd_sb.rearrange("p (s c f) -> p s c f", s=2, c=8)[:, q % 2, :, :],
                        in_=dsc[jb, q * 8:(q + 1) * 8, :, :].rearrange("c p f -> p c f")), 16)

            ld2n = LD(); mm_d = LD(); ve_d = LD(); st2n = LD()
            ent2 = LD()

            @phase("sync")
            def ldr(sy, ld2n=ld2n, ve_d=ve_d, st2n=st2n, base_tc=base_tc, ent2=ent2):
                ent2["v"] = s_ve.n
                ent2["oa"], ent2["ob"] = s_oa.n, s_ob.n
                sy.wait_ge(s_oa.h, ent2["oa"])
                sy.wait_ge(s_ob.h, ent2["ob"])
                done = []
                for jb in range(2):
                    for tci in range(4):
                        gi = jb * 4 + tci
                        tc = base_tc + tci
                        sy.wait_ge(s_ve.h, ve_d[gi - 4] if gi >= 4 else ent2["v"])
                        ld2n[gi] = slots[gi % 2].inc(sy.dma_start(
                            out=rbuf.rearrange("p (s w) -> p s w", s=4)[:, gi % 4, :],
                            in_=rsrc[tc * 128:(tc + 1) * 128, jb * 512:(jb + 1) * 512]), 16)
                        if gi >= 2:
                            gj = gi - 2
                            sy.wait_ge(s_ve.h, ve_d[gj])
                            jbj, tcj = gj // 4, gj % 4
                            st2n[gj] = oslots[gj % 2].inc(sy.dma_start(
                                out=rdst[(base_tc + tcj) * 128:(base_tc + tcj + 1) * 128,
                                         jbj * 512:(jbj + 1) * 512],
                                in_=robuf.rearrange("p (s h w) -> p s h w", s=2, h=2)[:, tcj % 2, jbj, :]), 16)
                            done.append(gj)
                for gj in range(8):
                    if gj not in done:
                        sy.wait_ge(s_ve.h, ve_d[gj])
                        jbj, tcj = gj // 4, gj % 4
                        st2n[gj] = oslots[gj % 2].inc(sy.dma_start(
                            out=rdst[(base_tc + tcj) * 128:(base_tc + tcj + 1) * 128,
                                     jbj * 512:(jbj + 1) * 512],
                            in_=robuf.rearrange("p (s h w) -> p s h w", s=2, h=2)[:, tcj % 2, jbj, :]), 16)

            @phase("tensor")
            def dmm(t, dld=dld, mm_d=mm_d, mm_dq=mm_dq, ve_d=ve_d, ev2n=ev2n):
                t.wait_ge(s_se.h, ev2n[15])
                for jb in range(2):
                    for q in range(4):
                        t.wait_ge(s_wl.h, dld[(jb, q)])
                        for tci in range(4):
                            if jb == 1 and q == 0:
                                t.wait_ge(s_ve.h, ve_d[tci])
                            psd = pq if tci < 2 else pk
                            bank = tci % 2
                            for j in range(8):
                                ic = q * 8 + j
                                ii = t.matmul(psd[:, bank * 512:bank * 512 + 512],
                                              x2T.rearrange("p (c n) -> p c n", c=32)[:, ic,
                                                                                      tci * 128:(tci + 1) * 128],
                                              wd_sb.rearrange("p (s c f) -> p s c f", s=2, c=8)[:, q % 2,
                                                                                                j, :],
                                              start=(ic == 0), stop=(ic == 31))
                            if tci == 3:
                                mm_dq[(jb, q)] = s_mm.inc(ii)
                                if q == 3:
                                    mm_d[(jb, 3)] = mm_dq[(jb, q)]
                            elif q == 3:
                                mm_d[(jb, tci)] = s_mm.inc(ii)

            @phase("vector")
            def dve(v, mm_d=mm_d, ve_d=ve_d, ld2n=ld2n, st2n=st2n,
                    base_tc=base_tc, fold_stats=fold_stats):
                for jb in range(2):
                    for tci in range(4):
                        gi = jb * 4 + tci
                        tc = base_tc + tci
                        v.wait_ge(slots[gi % 2].h, ld2n[gi])
                        if gi >= 2:
                            v.wait_ge(oslots[gi % 2].h, st2n[gi - 2])
                        v.wait_ge(s_mm.h, mm_d[(jb, tci)])
                        ro = robuf.rearrange("p (s h w) -> p s h w", s=2, h=2)[:, tci % 2, jb, :]
                        dsrc = pq if tci < 2 else pk
                        ii = v.scalar_tensor_tensor(
                            out=ro, in0=dsrc[:, (tci % 2) * 512:(tci % 2) * 512 + 512],
                            scalar=smisc[:, 12 + tci:13 + tci],
                            in1=rbuf.rearrange("p (s w) -> p s w", s=4)[:, gi % 4, :],
                            op0=ALU.mult, op1=ALU.add)
                        if fold_stats:
                            c0, c1 = (10, 11) if jb == 0 else (8, 9)
                            v.tensor_reduce(out=qcol[:, c0:c0 + 1], in_=ro, axis=AX.X, op=ALU.add)
                            v.tensor_tensor(out=sqt[:, 0:512], in0=ro, in1=ro, op=ALU.mult)
                            ii = v.tensor_reduce(out=qcol[:, c1:c1 + 1], in_=sqt[:, 0:512],
                                                 axis=AX.X, op=ALU.add)
                            if jb == 1:
                                nbar = s_ve.inc(ii)
                                v.wait_ge(s_ve.h, nbar)
                                v.tensor_add(out=stat_s[:, tc:tc + 1], in0=qcol[:, 10:11], in1=qcol[:, 8:9])
                                ii = v.tensor_add(out=stat_q[:, tc:tc + 1], in0=qcol[:, 11:12], in1=qcol[:, 9:10])
                        ve_d[gi] = s_ve.inc(ii)
            prev = {"u": mm_u, "d": mm_dq}
        fin = emit_stats_finish() if fold_stats else (lambda: s_se.n)
        return fin

    for l in range(2):
        emit_ffn_prep(l)
    for s in range(1 if dbg else S):
        rstd0 = emit_input_stats(s)
        rsrc0 = x_in[s]
        xh0 = emit_xhat(rsrc0, rstd0)
        qk0 = emit_qkv_l0(xh0)
        last = upto == "l0attn"
        rstd1 = emit_attn_out(0, qk0, rsrc0, y_out[s] if last else R[s], fold_stats=not last)
        if last:
            continue
        last = upto == "l0ffn"
        rstd2 = emit_ffn(0, rstd1, R[s], y_out[s] if last else R[s], fold_stats=not last)
        if last:
            continue
        xh1 = emit_xhat(R[s], rstd2)
        qk1 = emit_qkv_l1(xh1)
        last = upto == "l1attn"
        rstd3 = emit_attn_out(1, qk1, R[s], y_out[s] if last else R[s], fold_stats=not last)
        if last:
            continue
        emit_ffn(1, rstd3, R[s], y_out[s], fold_stats=False)

    if dbg:
        dgn = LD()

        @phase("vector")
        def dbg_cp(v):
            v.wait_ge(s_oa.h, s_oa.n)
            v.wait_ge(s_ob.h, s_ob.n)
            v.tensor_copy(out=rbuf[:, 0:16], in_=stat_s[:])
            v.tensor_copy(out=rbuf[:, 16:32], in_=stat_q[:])
            v.tensor_copy(out=rbuf[:, 32:48], in_=mean_all[:])
            v.tensor_copy(out=rbuf[:, 48:64], in_=rstd_all[:])
            v.tensor_copy(out=rbuf[:, 64:65], in_=bias_l0[:])
            v.tensor_copy(out=rbuf[0:1, 65:73], in_=sw1[:])
            v.tensor_copy(out=rbuf[0:1, 80:128], in_=gnr[0:1, 144:192])
            v.tensor_copy(out=rbuf[:, 128:192], in_=wqbc[:])
            v.tensor_copy(out=rbuf[:, 192:256], in_=wkbc[:])
            v.tensor_copy(out=rbuf[:, 256:1024], in_=qkhT[:, 0:768])
            dgn["v"] = s_ve.inc(v.tensor_copy(out=rbuf[:, 1024:2048], in_=xhT[:, 0:1024]))

        @phase("sync")
        def dbg_st(sy):
            sy.wait_ge(s_ve.h, dgn["v"])
            dgn["st"] = s_oa.inc(sy.dma_start(out=y_out[1, 0:128, :],
                                              in_=rbuf[:, 0:1024]), 16)
            dgn["st"] = s_oa.inc(sy.dma_start(out=y_out[1, 128:256, :],
                                              in_=rbuf[:, 1024:2048]), 16)

    fend = LD()

    @phase("sync")
    def fin_sync(sy):
        fend["a"], fend["b"] = s_oa.n, s_ob.n
        sy.wait_ge(s_oa.h, fend["a"])
        sy.wait_ge(s_ob.h, fend["b"])

    AUTO_SEMS = {"vector": s_ve, "scalar": s_se, "gpsimd": s_gp}

    def wrapped(eng, e, states):
        if eng in AUTO_SEMS:
            return AutoEng(e, states[eng], AUTO_SEMS[eng])
        return e

    _DRY[0] = True
    dry = DryEng()
    states = {k: {"live": [], "barriers": 0} for k in AUTO_SEMS}
    pass1 = []
    for eng, fn in PROG:
        fn(wrapped(eng, dry, states))
        pass1.append([sm.n for sm in SEMS])
    for sm in SEMS:
        sm.n = 0
    _DRY[0] = False
    states = {k: {"live": [], "barriers": 0} for k in AUTO_SEMS}
    blk = stack.enter_context(nc.Block())
    for pi, (eng, fn) in enumerate(PROG):
        getattr(blk, eng)(lambda e, fn=fn, eng=eng: fn(wrapped(eng, e, states)))
        now = [sm.n for sm in SEMS]
        if now != pass1[pi]:
            names = ["s_mm", "s_ve", "s_se", "s_gp", "s_xa", "s_xb", "s_oa", "s_ob", "s_wg", "s_ws", "s_wl"]
            diff = [(names[j], pass1[pi][j], now[j]) for j in range(len(now)) if now[j] != pass1[pi][j]]
            raise RuntimeError(f"pass divergence at phase {pi} ({eng}): {diff}")
    stack.close()
    return nc


def kernel(**inputs):
    t = lambda a: np.ascontiguousarray(np.asarray(a, dtype=np.float32).T)
    w = {
        "l0_wqT": t(inputs["L0_attn_Wq"]), "l0_wkT": t(inputs["L0_attn_Wk"]),
        "l0_wvT": t(inputs["L0_attn_Wv"]), "l0_woT": t(inputs["L0_attn_Wo"]),
        "l0_qnw": np.ascontiguousarray(np.asarray(inputs["L0_attn_qn_w"], dtype=np.float32).reshape(1, DH)),
        "l0_knw": np.ascontiguousarray(np.asarray(inputs["L0_attn_kn_w"], dtype=np.float32).reshape(1, DH)),
        "l1_vT": t(inputs["L1_attn_W_qkv_v"]), "l1_uT": t(inputs["L1_attn_W_qkv_u"]),
        "l1_ovT": t(inputs["L1_attn_W_o_v"]), "l1_ouT": t(inputs["L1_attn_W_o_u"]),
        "g0T": t(inputs["L0_ffn_w_gate"]), "u0T": t(inputs["L0_ffn_w_up"]),
        "d0T": t(inputs["L0_ffn_w_down"]),
        "g1T": t(inputs["L1_ffn_w_gate"]), "u1T": t(inputs["L1_ffn_w_up"]),
        "d1T": t(inputs["L1_ffn_w_down"]),
    }
    x = np.ascontiguousarray(np.asarray(inputs["L0_attn_x"], dtype=np.float32))
    nc = build_program("full")
    in_maps = []
    for c in range(NCORE):
        m = {"x": np.ascontiguousarray(x[c * S:(c + 1) * S])}
        m.update(w)
        in_maps.append(m)
    res = run_bass_kernel_spmd(nc, in_maps, core_ids=list(range(NCORE)))
    return np.concatenate([r["y"] for r in res.results], axis=0)
